# version 02 (final)
# baseline (speedup 1.0000x reference)
import numpy as np
import ml_dtypes
import concourse.bass as bass
import concourse.mybir as mybir
from concourse.bass_utils import run_bass_kernel_spmd
from contextlib import ExitStack

F32 = mybir.dt.float32
BF16 = mybir.dt.bfloat16
AF = mybir.ActivationFunctionType
ALU = mybir.AluOpType
I32 = mybir.dt.int32
F16 = mybir.dt.float16

NCORES = 8
T = 2048
H = 4096
I = 11008
HC = H // 128
ICN = I // 128
NTILE = T // 128
NIT = 11
ABASE = 16512
AVAIL = 212832
WD_GROUPS = [(0, 22), (22, 22), (44, 21), (65, 21)]
ENGS = ("tensor", "vector", "scalar", "gpsimd", "sync")

_CACHE = {}


class KB:
    def __init__(self, nc, es):
        self.nc = nc
        self.ops = {e: [] for e in ENGS}
        self.sem = {e: es.enter_context(nc.semaphore("C" + e[:2])) for e in ENGS}
        self.clock = {e: 0 for e in ENGS}
        self.hw = {e: {s: 0 for s in ENGS} for e in ENGS}
        self.bank_free = [None] * 8

    def wait(self, eng, waits):
        for w in waits:
            if w is None:
                continue
            src, val = w
            if val is None or val <= self.hw[eng][src]:
                continue
            self.hw[eng][src] = val
            self.ops[eng].append(lambda e, s=self.sem[src], v=val: e.wait_ge(s, v))

    def op(self, eng, fn, waits=(), inc=False, dma=False):
        self.wait(eng, waits)
        if inc:
            n = 16 if dma else 1
            self.clock[eng] += n
            self.ops[eng].append(lambda e, fn=fn, s=self.sem[eng], n=n: fn(e).then_inc(s, n))
            return self.clock[eng]
        self.ops[eng].append(fn)
        return None

    def chain(self, eng, fns, waits=()):
        t = None
        for i, fn in enumerate(fns):
            w = list(waits) if i == 0 else [(eng, t)]
            t = self.op(eng, fn, waits=w, inc=True)
        return t

    def use_banks(self, eng, banks):
        self.wait(eng, [self.bank_free[b] for b in banks])

    def free_banks(self, banks, eng, clk):
        for b in banks:
            self.bank_free[b] = (eng, clk)


def build_program(debug=False):
    dk = "ExternalOutput" if debug else "Internal"
    nc = bass.Bass("TRN2", target_bir_lowering=False)
    es = ExitStack()
    kb = KB(nc, es)

    d_xT = nc.dram_tensor("xT", [H, T], F32, kind="ExternalInput").ap()
    d_wg = nc.dram_tensor("wg", [ICN, 128, H], F32, kind="ExternalInput").ap()
    d_wu = nc.dram_tensor("wu", [ICN, 128, H], F32, kind="ExternalInput").ap()
    d_wd = nc.dram_tensor("wd", [HC, 128, I], F32, kind="ExternalInput").ap()
    d_Wt = nc.dram_tensor("Wt", [HC, 128, H], F32, kind="ExternalInput").ap()
    d_At = nc.dram_tensor("At", [2, 128, H], F32, kind="ExternalInput").ap()
    d_Bft = nc.dram_tensor("Bft", [2, 128, H], F32, kind="ExternalInput").ap()
    d_biasc = nc.dram_tensor("biasc", [128, 32], F32, kind="ExternalInput").ap()
    d_coef = nc.dram_tensor("coef", [16, 1], F32, kind="ExternalInput").ap()
    d_scal = nc.dram_tensor("scal", [16, 1], F32, kind="ExternalInput").ap()
    d_onesc = nc.dram_tensor("onesc", [128, 1], F32, kind="ExternalInput").ap()
    d_onescb = nc.dram_tensor("onescb", [128, 1], BF16, kind="ExternalInput").ap()
    d_onesr = nc.dram_tensor("onesr", [1, 128], F32, kind="ExternalInput").ap()
    d_rep = nc.dram_tensor("rep", [16, 256], F32, kind="ExternalInput").ap()
    d_ident = nc.dram_tensor("ident", [128, 128], F32, kind="ExternalInput").ap()

    d_out = nc.dram_tensor("outT", [H, T], F32, kind="ExternalOutput").ap()

    d_sfm = nc.dram_tensor("s_fm", [I, T], F16, kind=dk).ap()
    d_upfm = nc.dram_tensor("up_fm", [I, T], F16, kind=dk).ap()
    d_wub = nc.dram_tensor("wub", [ICN, 128, H], BF16, kind="Internal").ap()
    d_thd = nc.dram_tensor("thd", [16, 128], F32, kind="Internal").ap()
    d_th = nc.dram_tensor("th", [128, 16], F32, kind=dk).ap() if debug else None
    d_dbg = nc.dram_tensor("dbg", [128, NTILE * NIT * 4], F32, kind=dk).ap() if debug else None
    d_x1 = nc.dram_tensor("x1", [H, T], BF16, kind=dk).ap()

    def sb(name, shape, dtype, off):
        nbytes = int(np.prod(shape[1:])) * (2 if dtype in (BF16, F16) else 4)
        assert off % 32 == 0 and off + nbytes <= AVAIL, f"{name}: {off}+{nbytes}"
        return nc.alloc_sbuf_tensor_at(name, shape, dtype, offset=ABASE + off).ap()

    LNxT = sb("LNxT", [128, HC * T], BF16, 0)
    dTr = sb("dTr", [128, ICN * 1024], BF16, 0)
    LNyT = sb("LNyT", [128, HC * T], BF16, 0)

    xst = [sb(f"xst{i}", [128, T], F32, 131072 + i * 8192) for i in range(2)]
    sqst = [sb(f"sqst{i}", [128, T], F32, 147456 + i * 8192) for i in range(2)]
    m_row = sb("m_row", [1, T], F32, 163840)
    s2_row = sb("s2_row", [1, T], F32, 172032)
    tmp_row = sb("tmp_row", [1, T], F32, 180224)
    r_bc = sb("r_bc", [128, T], BF16, 188416)
    mr_bc = sb("mr_bc", [128, T], BF16, 192512)

    wgbuf = [sb(f"wgbuf{i}", [128, H], BF16, 131072 + i * 8192) for i in range(2)]
    sst = [sb(f"sst{i}", [128, T], F16, 147456 + i * 4096) for i in range(2)]

    wubuf = [sb(f"wubuf{i}", [128, H], BF16, 131072 + i * 8192) for i in range(2)]
    ust = [sb(f"ust{i}", [128, T], F16, 147456 + i * 4096) for i in range(2)]
    lane_s = sb("lane_s", [128, I], F16, 155648)
    lane_a = sb("lane_a", [128, I], F16, 177664)

    wdbuf = [sb(f"wdbuf{i}", [128, 22 * 128], BF16, 176128 + i * 5632) for i in range(2)]
    th_bc = sb("th_bc", [128, 1024], F32, 187392)
    sst2 = [sb(f"sst2{i}", [128, 1024], F16, 191488 + i * 2048) for i in range(2)]
    ust2 = [sb(f"ust2{i}", [128, 1024], F16, 195584 + i * 2048) for i in range(2)]
    ast2 = [sb(f"ast2{i}", [128, 1024], F16, 199680 + i * 2048) for i in range(2)]
    throw = sb("throw", [1, 1024], F32, 203776)
    dbglog = sb("dbglog", [128, NTILE * NIT * 4], F32, 199680)
    xst3 = sb("xst3", [128, 1024], F32, 203776)
    x1st = sb("x1st", [128, 1024], BF16, 207872)

    x1st4 = [sb(f"x1st4{i}", [128, T], BF16, 131072 + i * 4096) for i in range(2)]
    sq4 = [sb(f"sq4{i}", [128, T], BF16, 139264 + i * 4096) for i in range(2)]
    r2_bc = sb("r2_bc", [128, T], BF16, 147456)
    mr2_bc = sb("mr2_bc", [128, T], BF16, 151552)
    m2_row = sb("m2_row", [1, T], F32, 155648)
    s22_row = sb("s22_row", [1, T], F32, 163840)
    tmp2_row = sb("tmp2_row", [1, T], F32, 172032)
    sd2_row = sb("sd2_row", [1, T], F32, 180224)
    ir2_bc = sb("ir2_bc", [128, T], F32, 188416)
    m2_bc = sb("m2_bc", [128, T], F32, 163840)
    A_sb = [sb(f"A_sb{i}", [128, H], BF16, 131072 + i * 8192) for i in range(2)]
    xaT = [sb(f"xaT{i}", [128, T], BF16, 147456 + i * 4096) for i in range(2)]
    Wbuf = [sb(f"Wbuf{i}", [128, H], BF16, 131072 + i * 8192) for i in range(2)]
    Bf_sb = [sb("Bf_sb0", [128, H], BF16, 155648), sb("Bf_sb1", [128, H], BF16, 180224)]
    outst = sb("outst", [128, T], F32, 172032)

    to = [209920]

    def tail(name, shape, dtype):
        t = sb(name, shape, dtype, to[0])
        nbytes = int(np.prod(shape[1:])) * (2 if dtype in (BF16, F16) else 4)
        to[0] += (nbytes + 31) // 32 * 32
        return t

    ones_c = tail("ones_c", [128, 1], F32)
    ones_cb = tail("ones_cb", [128, 1], BF16)
    ones_r = tail("ones_r", [1, 128], F32)
    rep_sb = tail("rep_sb", [16, 256], F32)
    biasc_sb = tail("biasc_sb", [128, 32], F32)
    coef_sb = tail("coef_sb", [16, 1], F32)
    scal_sb = tail("scal_sb", [16, 1], F32)
    w16 = tail("w16", [16, 1], F32)
    wexp = tail("wexp", [128, 2], F32)
    b_lo = tail("b_lo", [128, 1], F32)
    b_hi = tail("b_hi", [128, 1], F32)
    b_flo = tail("b_flo", [128, 1], F32)
    b_fhi = tail("b_fhi", [128, 1], F32)
    b_th = tail("b_th", [128, 1], F32)
    b_nth = tail("b_nth", [128, 1], F32)
    b_cs = tail("b_cs", [128, 1], F32)
    b_m = tail("b_m", [128, 1], I32)
    b_d = tail("b_d", [128, 1], F32)
    b_den = tail("b_den", [128, 1], F32)
    b_rd = tail("b_rd", [128, 1], F32)
    b_t1 = tail("b_t1", [128, 1], F32)
    eps1 = tail("eps1", [1, 1], F32)
    th_all = tail("th_all", [128, 16], F32)
    b_cs2 = tail("b_cs2", [128, 1], F32)
    ident_sb = tail("ident_sb", [128, 128], F32)
    thT = sb("thT", [16, 128], F32, 187392)
    assert to[0] <= AVAIL, to[0]

    pb = [nc.alloc_psum_tensor(f"pb{i}", [128, 512], F32).ap() for i in range(8)]

    ld0 = None
    for dst, src in ((ones_c, d_onesc), (ones_cb, d_onescb), (ones_r, d_onesr),
                     (rep_sb, d_rep), (biasc_sb, d_biasc), (coef_sb, d_coef),
                     (scal_sb, d_scal), (ident_sb, d_ident)):
        ld0 = kb.op("sync", lambda e, d=dst, s=src: e.dma_start(d[:, :], s[:, :]),
                    inc=True, dma=True)

    v_eps = kb.op("vector", lambda e: e.memset(eps1[:, :], 1e-5), inc=True)

    sq_clk = [None, None]
    te_clk = [None, None]
    for hc in range(HC):
        sl = hc % 2
        w = [("sync", ld0)]
        if hc >= 2:
            w = [("tensor", te_clk[sl]), ("scalar", sq_clk[sl])]
        ldc = kb.op("sync", lambda e, hc=hc, sl=sl: e.dma_start(
            xst[sl][:, :], d_xT[hc * 128:(hc + 1) * 128, :]), waits=w, inc=True, dma=True)
        sq_clk[sl] = kb.op("scalar", lambda e, sl=sl: e.activation(
            sqst[sl][:, :], xst[sl][:, :], AF.Square),
            waits=[("sync", ldc)] + ([("tensor", te_clk[sl])] if hc >= 2 else []),
            inc=True)
        if hc == 0:
            kb.use_banks("tensor", range(8))
        kb.wait("tensor", [("sync", ldc)])
        for t in range(4):
            kb.op("tensor", lambda e, sl=sl, t=t, hc=hc: e.matmul(
                pb[t][0:1, :], ones_c[:, 0:1], xst[sl][:, t * 512:(t + 1) * 512],
                start=(hc == 0), stop=(hc == HC - 1)))
        kb.wait("tensor", [("scalar", sq_clk[sl])])
        for t in range(4):
            clk = kb.op("tensor", lambda e, sl=sl, t=t, hc=hc: e.matmul(
                pb[4 + t][0:1, :], ones_c[:, 0:1], sqst[sl][:, t * 512:(t + 1) * 512],
                start=(hc == 0), stop=(hc == HC - 1)), inc=(t == 3))
        te_clk[sl] = clk
    te_stat = te_clk[(HC - 1) % 2]

    fns = [lambda e, t=t: e.tensor_scalar(
        m_row[0:1, t * 512:(t + 1) * 512], pb[t][0:1, :], 1.0 / H, None, ALU.mult)
        for t in range(4)]
    fns += [lambda e, t=t: e.tensor_scalar(
        s2_row[0:1, t * 512:(t + 1) * 512], pb[4 + t][0:1, :], 1.0 / H, None, ALU.mult)
        for t in range(4)]
    fns.append(lambda e: e.tensor_tensor(tmp_row[0:1, :], m_row[0:1, :], m_row[0:1, :], ALU.mult))
    fns.append(lambda e: e.tensor_tensor(s2_row[0:1, :], s2_row[0:1, :], tmp_row[0:1, :], ALU.subtract))
    v_var = kb.chain("vector", fns, waits=[("tensor", te_stat)])
    kb.free_banks(range(8), "vector", v_var)

    a_sd = kb.op("scalar", lambda e: e.activation(
        tmp_row[0:1, :], s2_row[0:1, :], AF.Sqrt, bias=eps1[0:1, 0:1]),
        waits=[("vector", v_var), ("vector", v_eps)], inc=True)
    v_mr = kb.chain("vector", [
        lambda e: e.reciprocal(s2_row[0:1, :], tmp_row[0:1, :]),
        lambda e: e.tensor_tensor(tmp_row[0:1, :], m_row[0:1, :], s2_row[0:1, :], ALU.mult),
    ], waits=[("scalar", a_sd)])

    kb.use_banks("tensor", range(8))
    kb.wait("tensor", [("vector", v_mr)])
    te_bc = None
    for t in range(4):
        kb.op("tensor", lambda e, t=t: e.matmul(
            pb[t][:, :], ones_r[0:1, :], s2_row[0:1, t * 512:(t + 1) * 512],
            start=True, stop=True))
    for t in range(4):
        te_bc = kb.op("tensor", lambda e, t=t: e.matmul(
            pb[4 + t][:, :], ones_r[0:1, :], tmp_row[0:1, t * 512:(t + 1) * 512],
            start=True, stop=True), inc=(t == 3))
    fns = [lambda e, t=t: e.tensor_copy(r_bc[:, t * 512:(t + 1) * 512], pb[t][:, :])
           for t in range(4)]
    fns += [lambda e, t=t: e.tensor_copy(mr_bc[:, t * 512:(t + 1) * 512], pb[4 + t][:, :])
            for t in range(4)]
    v_bc = kb.chain("vector", fns, waits=[("tensor", te_bc)])
    kb.free_banks(range(8), "vector", v_bc)

    ap_clk = [None, None]
    for hc in range(HC):
        sl = hc % 2
        w = [("vector", ap_clk[sl])] if hc >= 2 else [("tensor", te_stat), ("scalar", sq_clk[sl])]
        ldc = kb.op("sync", lambda e, hc=hc, sl=sl: e.dma_start(
            xst[sl][:, :], d_xT[hc * 128:(hc + 1) * 128, :]), waits=w, inc=True, dma=True)
        ap_clk[sl] = kb.chain("vector", [
            lambda e, sl=sl: e.tensor_tensor(sqst[sl][:, :], xst[sl][:, :], r_bc[:, :], ALU.mult),
            lambda e, sl=sl, hc=hc: e.tensor_tensor(
                LNxT[:, hc * T:(hc + 1) * T], sqst[sl][:, :], mr_bc[:, :], ALU.subtract),
        ], waits=[("sync", ldc), ("vector", v_bc)])
    ln1_done = ap_clk[(HC - 1) % 2]

    wu_conv = []
    wld = [None, None]
    grp = [None, None]
    ev = [None, None]
    st = [None, None]
    s_st = []
    for ic in range(ICN):
        sl = ic % 2
        qb = [4 * sl, 4 * sl + 1, 4 * sl + 2, 4 * sl + 3]
        w = [("tensor", grp[sl])] if ic >= 2 else [("vector", ln1_done)]
        wld[sl] = kb.op("gpsimd", lambda e, ic=ic, sl=sl: e.dma_start(
            wgbuf[sl][:, :], d_wg[ic, :, :]), waits=w, inc=True, dma=True)
        wu_conv.append(kb.op("gpsimd", lambda e, ic=ic: e.dma_start(
            d_wub[ic, :, :], d_wu[ic, :, :]), inc=True, dma=True))
        kb.wait("tensor", [("gpsimd", wld[sl])] + ([("vector", ln1_done)] if ic < 2 else []))
        kb.use_banks("tensor", qb)
        g = None
        for hc in range(HC):
            for t in range(4):
                g = kb.op("tensor", lambda e, sl=sl, hc=hc, t=t: e.matmul(
                    pb[4 * sl + t][:, :], wgbuf[sl][:, hc * 128:(hc + 1) * 128],
                    LNxT[:, hc * T + t * 512: hc * T + (t + 1) * 512],
                    start=(hc == 0), stop=(hc == HC - 1)),
                    inc=(hc == HC - 1 and t == 3))
        grp[sl] = g
        wse = [("tensor", g)] + ([("sync", st[sl])] if st[sl] else [("vector", ln1_done)])
        ee = None
        for t in range(4):
            ee = kb.op("scalar", lambda e, sl=sl, t=t: e.activation(
                sst[sl][:, t * 512:(t + 1) * 512], pb[4 * sl + t][:, :], AF.Silu),
                waits=wse if t == 0 else [("scalar", ee)], inc=True)
        ev[sl] = ee
        kb.free_banks(qb, "scalar", ee)
        st[sl] = kb.op("sync", lambda e, ic=ic, sl=sl: e.dma_start(
            d_sfm[ic * 128:(ic + 1) * 128, :], sst[sl][:, :]),
            waits=[("scalar", ee)], inc=True, dma=True)
        s_st.append(st[sl])
    te_gate = grp[(ICN - 1) % 2]

    bis = {"lastcnt": None, "upd": None}

    def bisect_gen():
        for j in range(NTILE):
            w = [("sync", s_st[-1])]
            if bis["lastcnt"] is not None:
                w.append(("scalar", bis["lastcnt"]))
            if bis["upd"] is not None:
                w.append(("vector", bis["upd"]))
            tld = kb.op("sync", lambda e, j=j: e.dma_start_transpose(
                lane_s[:, :], d_sfm[:, j * 128:(j + 1) * 128]), waits=w, inc=True, dma=True)
            iw = [("sync", tld)]
            upd = kb.chain("vector", [
                lambda e: e.tensor_scalar(lane_a[:, :], lane_s[:, :], -1.0, None, ALU.mult),
                lambda e: e.tensor_tensor(lane_a[:, :], lane_a[:, :], lane_s[:, :], ALU.max),
                lambda e: e.memset(b_lo[:, :], 0.05),
                lambda e: e.memset(b_hi[:, :], 1.0),
                lambda e: e.memset(b_flo[:, :], 11008.0),
                lambda e: e.memset(b_fhi[:, :], -11008.0),
                lambda e: e.memset(b_th[:, :], 0.28),
                lambda e: e.memset(b_nth[:, :], -0.28),
            ], waits=iw)
            yield
            for k in range(NIT):
                kb.op("scalar", lambda e: e.activation(
                    lane_s[:, :], lane_a[:, :], AF.Sign, bias=b_nth[:, 0:1],
                    accum_out=b_cs[:, 0:1]), waits=[("vector", upd)])
                cnt = kb.op("scalar", lambda e: e.copy(
                    b_cs2[:, :], b_cs[:, :]), inc=True)
                bis["lastcnt"] = cnt
                dbgops = []
                if debug:
                    li = (j * NIT + k) * 4
                    dbgops = [
                        lambda e, li=li: e.tensor_copy(dbglog[:, li:li + 1], b_cs2[:, :]),
                        lambda e, li=li: e.tensor_copy(dbglog[:, li + 1:li + 2], b_th[:, :]),
                        lambda e, li=li: e.tensor_copy(dbglog[:, li + 2:li + 3], b_flo[:, :]),
                        lambda e, li=li: e.tensor_copy(dbglog[:, li + 3:li + 4], b_fhi[:, :]),
                    ]
                upd = kb.chain("vector", dbgops + [
                    lambda e: e.tensor_scalar(b_m[:, :], b_cs2[:, :], 0.0, None, ALU.is_gt),
                    lambda e: e.select(b_lo[:, :], b_m[:, :], b_th[:, :], b_lo[:, :]),
                    lambda e: e.select(b_hi[:, :], b_m[:, :], b_hi[:, :], b_th[:, :]),
                    lambda e: e.tensor_scalar(b_t1[:, :], b_flo[:, :], 0.5, None, ALU.mult),
                    lambda e: e.select(b_flo[:, :], b_m[:, :], b_cs2[:, :], b_t1[:, :]),
                    lambda e: e.tensor_scalar(b_t1[:, :], b_fhi[:, :], 0.5, None, ALU.mult),
                    lambda e: e.select(b_fhi[:, :], b_m[:, :], b_t1[:, :], b_cs2[:, :]),
                    lambda e: e.tensor_tensor(b_d[:, :], b_hi[:, :], b_lo[:, :], ALU.subtract),
                    lambda e: e.tensor_tensor(b_den[:, :], b_fhi[:, :], b_flo[:, :], ALU.subtract),
                    lambda e: e.tensor_scalar(b_den[:, :], b_den[:, :], -1e-30, None, ALU.min),
                    lambda e: e.reciprocal(b_rd[:, :], b_den[:, :]),
                    lambda e: e.scalar_tensor_tensor(
                        b_t1[:, :], b_flo[:, :], b_rd[:, 0:1], b_d[:, :], ALU.mult, ALU.mult),
                    lambda e: e.tensor_tensor(b_th[:, :], b_lo[:, :], b_t1[:, :], ALU.subtract),
                    lambda e: e.tensor_tensor(b_th[:, :], b_th[:, :], b_lo[:, :], ALU.max),
                    lambda e: e.tensor_tensor(b_th[:, :], b_th[:, :], b_hi[:, :], ALU.min),
                    lambda e: e.tensor_scalar(b_nth[:, :], b_th[:, :], -1.0, None, ALU.mult),
                ], waits=[("scalar", cnt)])
                yield
            upd = kb.chain("vector", [
                lambda e, j=j: e.tensor_copy(th_all[:, j:j + 1], b_hi[:, :]),
            ], waits=[("vector", upd)])
            bis["upd"] = upd
            yield

    bgen = bisect_gen()
    bsteps = NTILE * (NIT + 2)
    bdone = [False]
    pump_acc = [0.0]

    def pump():
        pump_acc[0] += bsteps / ICN
        while pump_acc[0] >= 1.0 and not bdone[0]:
            pump_acc[0] -= 1.0
            try:
                next(bgen)
            except StopIteration:
                bdone[0] = True

    wldu = [None, None]
    grpu = [None, None]
    stu = [None, None]
    up_st = []
    for ic in range(ICN):
        sl = ic % 2
        qb = [4 * sl, 4 * sl + 1, 4 * sl + 2, 4 * sl + 3]
        w = [("tensor", grpu[sl])] if ic >= 2 else [("tensor", te_gate)]
        w.append(("gpsimd", wu_conv[ic]))
        wldu[sl] = kb.op("sync", lambda e, ic=ic, sl=sl: e.dma_start(
            wubuf[sl][:, :], d_wub[ic, :, :]), waits=w, inc=True, dma=True)
        kb.wait("tensor", [("sync", wldu[sl])])
        kb.use_banks("tensor", qb)
        g = None
        for hc in range(HC):
            for t in range(4):
                g = kb.op("tensor", lambda e, sl=sl, hc=hc, t=t: e.matmul(
                    pb[4 * sl + t][:, :], wubuf[sl][:, hc * 128:(hc + 1) * 128],
                    LNxT[:, hc * T + t * 512: hc * T + (t + 1) * 512],
                    start=(hc == 0), stop=(hc == HC - 1)),
                    inc=(hc == HC - 1 and t == 3))
        grpu[sl] = g
        wse = [("tensor", g)]
        wse.append(("sync", stu[sl]) if stu[sl] else ("sync", s_st[-1]))
        ee = None
        for t in range(4):
            ee = kb.op("vector", lambda e, sl=sl, t=t: e.tensor_copy(
                ust[sl][:, t * 512:(t + 1) * 512], pb[4 * sl + t][:, :]),
                waits=wse if t == 0 else [("vector", ee)], inc=True)
        kb.free_banks(qb, "vector", ee)
        stu[sl] = kb.op("sync", lambda e, ic=ic, sl=sl: e.dma_start(
            d_upfm[ic * 128:(ic + 1) * 128, :], ust[sl][:, :]),
            waits=[("vector", ee)], inc=True, dma=True)
        up_st.append(stu[sl])
        pump()
    while not bdone[0]:
        try:
            next(bgen)
        except StopIteration:
            bdone[0] = True
    te_up = grpu[(ICN - 1) % 2]
    dbst = None
    if debug:
        dbst = kb.op("sync", lambda e: e.dma_start(d_dbg[:, :], dbglog[:, :]),
                     waits=[("vector", bis["upd"]), ("scalar", bis["lastcnt"])],
                     inc=True, dma=True)
        dbst = kb.op("sync", lambda e: e.dma_start(d_th[:, :], th_all[:, :]),
                     waits=[("vector", bis["upd"])], inc=True, dma=True)

    dn_final = None
    xevac = None
    x1s = None
    wld3 = [None, None]
    gclk = [None, None]
    gi = 0
    kb.use_banks("tensor", [6])
    t_tr = kb.op("tensor", lambda e: e.transpose(
        pb[6][0:16, 0:128], th_all[:, :], ident_sb[:, :]),
        waits=[("vector", bis["upd"]), ("sync", ld0)], inc=True)
    v_tr = kb.chain("vector", [
        lambda e: e.tensor_copy(thT[0:16, :], pb[6][0:16, 0:128]),
    ], waits=[("tensor", t_tr)])
    kb.free_banks([6], "vector", v_tr)
    thd_st = kb.op("sync", lambda e: e.dma_start(d_thd[:, :], thT[0:16, :]),
                   waits=[("vector", v_tr)], inc=True, dma=True)
    for h in range(2):
        hs = slice(h * 1024, (h + 1) * 1024)
        thl = kb.op("sync", lambda e, h=h: e.dma_start(
            throw[0:1, :], d_thd.rearrange("a b -> (a b)")[None, h * 1024:(h + 1) * 1024]),
            waits=[("sync", thd_st)] + ([("vector", xevac)] if xevac else []),
            inc=True, dma=True)
        kb.use_banks("tensor", [4, 5])
        tbm = None
        for t in range(2):
            tbm = kb.op("tensor", lambda e, t=t: e.matmul(
                pb[4 + t][:, :], ones_r[0:1, :], throw[0:1, t * 512:(t + 1) * 512],
                start=True, stop=True), waits=[("sync", thl)], inc=(t == 1))
        thbc = kb.chain("vector", [
            lambda e, t=t: e.tensor_copy(th_bc[:, t * 512:(t + 1) * 512], pb[4 + t][:, :])
            for t in range(2)], waits=[("tensor", tbm)])
        kb.free_banks([4, 5], "vector", thbc)

        sld = [None, None]
        uld = [None, None]
        cmb = [None, None]
        ab = [None, None]
        for ic in range(ICN):
            sl = ic % 2
            if ic >= 2:
                w = [("vector", cmb[sl])]
            elif h == 0:
                w = [("tensor", te_up), ("vector", thbc)]
            else:
                w = [("tensor", dn_final), ("vector", thbc)]
            sld[sl] = kb.op("sync", lambda e, ic=ic, sl=sl, hs=hs: e.dma_start(
                sst2[sl][:, :], d_sfm[ic * 128:(ic + 1) * 128, hs]), waits=w, inc=True, dma=True)
            uld[sl] = kb.op("sync", lambda e, ic=ic, sl=sl, hs=hs: e.dma_start(
                ust2[sl][:, :], d_upfm[ic * 128:(ic + 1) * 128, hs]),
                waits=[("sync", up_st[ic])], inc=True, dma=True)
            ab[sl] = kb.op("scalar", lambda e, sl=sl: e.activation(
                ast2[sl][:, :], sst2[sl][:, :], AF.Abs),
                waits=[("sync", sld[sl])] + ([("vector", cmb[sl])] if ic >= 2 else []),
                inc=True)
            cmb[sl] = kb.chain("vector", [
                lambda e, sl=sl: e.tensor_tensor(
                    ast2[sl][:, :], ast2[sl][:, :], th_bc[:, :], ALU.is_gt),
                lambda e, sl=sl: e.tensor_tensor(
                    sst2[sl][:, :], sst2[sl][:, :], ast2[sl][:, :], ALU.mult),
                lambda e, sl=sl, ic=ic: e.tensor_tensor(
                    dTr[:, ic * 1024:(ic + 1) * 1024], sst2[sl][:, :], ust2[sl][:, :], ALU.mult),
            ], waits=[("scalar", ab[sl]), ("sync", uld[sl]), ("vector", thbc)])
        prep_final = cmb[(ICN - 1) % 2]

        for hc in range(HC):
            pbks = [2 * (hc % 2), 2 * (hc % 2) + 1]
            g0 = None
            for g, (off, cnt) in enumerate(WD_GROUPS):
                sl3 = gi % 2
                if gi >= 2:
                    w = [("tensor", gclk[sl3])]
                else:
                    w = [("vector", bis["upd"]), ("scalar", bis["lastcnt"])]
                wld3[sl3] = kb.op("gpsimd", lambda e, hc=hc, off=off, cnt=cnt, sl3=sl3: e.dma_start(
                    wdbuf[sl3][:, 0:cnt * 128], d_wd[hc, :, off * 128:(off + cnt) * 128]),
                    waits=w, inc=True, dma=True)
                kb.wait("tensor", [("gpsimd", wld3[sl3]), ("vector", prep_final)])
                if g == 0:
                    kb.use_banks("tensor", pbks)
                for j in range(cnt):
                    for t in range(2):
                        g0 = kb.op("tensor", lambda e, sl3=sl3, j=j, t=t, off=off, hc=hc: e.matmul(
                            pb[2 * (hc % 2) + t][:, :],
                            wdbuf[sl3][:, j * 128:(j + 1) * 128],
                            dTr[:, (off + j) * 1024 + t * 512:(off + j) * 1024 + (t + 1) * 512],
                            start=(off + j == 0), stop=(off + j == ICN - 1)),
                            inc=(j == cnt - 1 and t == 1))
                gclk[sl3] = g0
                gi += 1
            dn_final = g0
            xl = kb.op("sync", lambda e, hc=hc, hs=hs: e.dma_start(
                xst3[:, :], d_xT[hc * 128:(hc + 1) * 128, hs]),
                waits=([("vector", xevac)] if xevac else []) + [("vector", thbc)],
                inc=True, dma=True)
            xevac = kb.chain("vector", [
                lambda e, hc=hc: e.tensor_tensor(
                    x1st[:, 0:512], pb[2 * (hc % 2)][:, :], xst3[:, 0:512], ALU.add),
                lambda e, hc=hc: e.tensor_tensor(
                    x1st[:, 512:1024], pb[2 * (hc % 2) + 1][:, :], xst3[:, 512:1024], ALU.add),
            ], waits=[("tensor", dn_final), ("sync", xl)] + ([("sync", x1s)] if x1s else []))
            kb.free_banks(pbks, "vector", xevac)
            x1s = kb.op("sync", lambda e, hc=hc, hs=hs: e.dma_start(
                d_x1[hc * 128:(hc + 1) * 128, hs], x1st[:, :]),
                waits=[("vector", xevac)], inc=True, dma=True)
    te_down = dn_final
    x1_last = x1s

    sq4_clk = [None, None]
    te4_clk = [None, None]
    for hc in range(HC):
        sl = hc % 2
        w = [("sync", x1_last)]
        if hc >= 2:
            w += [("tensor", te4_clk[sl]), ("scalar", sq4_clk[sl])]
        else:
            w += [("tensor", te_down)]
        ldc = kb.op("sync", lambda e, hc=hc, sl=sl: e.dma_start(
            x1st4[sl][:, :], d_x1[hc * 128:(hc + 1) * 128, :]), waits=w, inc=True, dma=True)
        sq4_clk[sl] = kb.op("scalar", lambda e, sl=sl: e.activation(
            sq4[sl][:, :], x1st4[sl][:, :], AF.Square),
            waits=[("sync", ldc)] + ([("tensor", te4_clk[sl])] if hc >= 2 else []),
            inc=True)
        if hc == 0:
            kb.use_banks("tensor", range(8))
        kb.wait("tensor", [("sync", ldc)])
        for t in range(4):
            kb.op("tensor", lambda e, sl=sl, t=t, hc=hc: e.matmul(
                pb[t][0:1, :], ones_cb[:, 0:1], x1st4[sl][:, t * 512:(t + 1) * 512],
                start=(hc == 0), stop=(hc == HC - 1)))
        kb.wait("tensor", [("scalar", sq4_clk[sl])])
        for t in range(4):
            clk = kb.op("tensor", lambda e, sl=sl, t=t, hc=hc: e.matmul(
                pb[4 + t][0:1, :], ones_cb[:, 0:1], sq4[sl][:, t * 512:(t + 1) * 512],
                start=(hc == 0), stop=(hc == HC - 1)), inc=(t == 3))
        te4_clk[sl] = clk
    te_stat4 = te4_clk[(HC - 1) % 2]

    fns = [lambda e, t=t: e.tensor_scalar(
        m2_row[0:1, t * 512:(t + 1) * 512], pb[t][0:1, :], 1.0 / H, None, ALU.mult)
        for t in range(4)]
    fns += [lambda e, t=t: e.tensor_scalar(
        s22_row[0:1, t * 512:(t + 1) * 512], pb[4 + t][0:1, :], 1.0 / H, None, ALU.mult)
        for t in range(4)]
    fns.append(lambda e: e.tensor_tensor(tmp2_row[0:1, :], m2_row[0:1, :], m2_row[0:1, :], ALU.mult))
    fns.append(lambda e: e.tensor_tensor(s22_row[0:1, :], s22_row[0:1, :], tmp2_row[0:1, :], ALU.subtract))
    v_var4 = kb.chain("vector", fns, waits=[("tensor", te_stat4)])
    kb.free_banks(range(8), "vector", v_var4)
    a_sd4 = kb.op("scalar", lambda e: e.activation(
        sd2_row[0:1, :], s22_row[0:1, :], AF.Sqrt, bias=eps1[0:1, 0:1]),
        waits=[("vector", v_var4)], inc=True)
    v_mr4 = kb.chain("vector", [
        lambda e: e.reciprocal(tmp2_row[0:1, :], sd2_row[0:1, :]),
        lambda e: e.tensor_tensor(s22_row[0:1, :], m2_row[0:1, :], tmp2_row[0:1, :], ALU.mult),
    ], waits=[("scalar", a_sd4)])
    kb.use_banks("tensor", range(8))
    kb.wait("tensor", [("vector", v_mr4)])
    te_bc4a = None
    for t in range(4):
        kb.op("tensor", lambda e, t=t: e.matmul(
            pb[t][:, :], ones_r[0:1, :], tmp2_row[0:1, t * 512:(t + 1) * 512], start=True, stop=True))
    for t in range(4):
        te_bc4a = kb.op("tensor", lambda e, t=t: e.matmul(
            pb[4 + t][:, :], ones_r[0:1, :], s22_row[0:1, t * 512:(t + 1) * 512],
            start=True, stop=True), inc=(t == 3))
    fns = [lambda e, t=t: e.tensor_copy(r2_bc[:, t * 512:(t + 1) * 512], pb[t][:, :])
           for t in range(4)]
    fns += [lambda e, t=t: e.tensor_copy(mr2_bc[:, t * 512:(t + 1) * 512], pb[4 + t][:, :])
            for t in range(4)]
    v_bc4a = kb.chain("vector", fns, waits=[("tensor", te_bc4a)])
    kb.free_banks(range(8), "vector", v_bc4a)
    kb.use_banks("tensor", range(8))
    te_bc4b = None
    for t in range(4):
        kb.op("tensor", lambda e, t=t: e.matmul(
            pb[t][:, :], ones_r[0:1, :], sd2_row[0:1, t * 512:(t + 1) * 512], start=True, stop=True))
    for t in range(4):
        te_bc4b = kb.op("tensor", lambda e, t=t: e.matmul(
            pb[4 + t][:, :], ones_r[0:1, :], m2_row[0:1, t * 512:(t + 1) * 512],
            start=True, stop=True), inc=(t == 3))
    fns = [lambda e, t=t: e.tensor_copy(ir2_bc[:, t * 512:(t + 1) * 512], pb[t][:, :])
           for t in range(4)]
    fns += [lambda e, t=t: e.tensor_copy(m2_bc[:, t * 512:(t + 1) * 512], pb[4 + t][:, :])
            for t in range(4)]
    v_bc4b = kb.chain("vector", fns, waits=[("tensor", te_bc4b)])
    kb.free_banks(range(8), "vector", v_bc4b)

    ap4 = [None, None]
    for hc in range(HC):
        sl = hc % 2
        w = [("vector", ap4[sl])] if hc >= 2 else [("tensor", te_stat4), ("scalar", sq4_clk[sl])]
        ldc = kb.op("sync", lambda e, hc=hc, sl=sl: e.dma_start(
            x1st4[sl][:, :], d_x1[hc * 128:(hc + 1) * 128, :]), waits=w, inc=True, dma=True)
        ap4[sl] = kb.chain("vector", [
            lambda e, sl=sl: e.tensor_tensor(outst[:, :], x1st4[sl][:, :], r2_bc[:, :], ALU.mult),
            lambda e, sl=sl, hc=hc: e.tensor_tensor(
                LNyT[:, hc * T:(hc + 1) * T], outst[:, :], mr2_bc[:, :], ALU.subtract),
        ], waits=[("sync", ldc), ("vector", v_bc4a)])
    ln2_done = ap4[(HC - 1) % 2]

    pl_w16 = kb.op("gpsimd", lambda e: e.tensor_tensor(
        w16[:, :], coef_sb[:, :], scal_sb[:, :], ALU.mult),
        waits=[("sync", ld0)], inc=True)
    kb.use_banks("tensor", [0, 1])
    te_rep = None
    for c in range(2):
        te_rep = kb.op("tensor", lambda e, c=c: e.matmul(
            pb[c][:, 0:1], rep_sb[:, c * 128:(c + 1) * 128], w16[:, 0:1],
            start=True, stop=True), waits=[("gpsimd", pl_w16)], inc=(c == 1))
    v_wexp = kb.chain("vector", [
        lambda e, c=c: e.tensor_copy(wexp[:, c:c + 1], pb[c][:, 0:1]) for c in range(2)
    ], waits=[("tensor", te_rep)])
    kb.free_banks([0, 1], "vector", v_wexp)

    a_ld = [kb.op("gpsimd", lambda e, c=c: e.dma_start(
        A_sb[c][:, :], d_At[c, :, :]), waits=[("vector", ln2_done)], inc=True, dma=True)
        for c in range(2)]
    kb.use_banks("tensor", range(8))
    te_xa = None
    for hc in range(HC):
        if hc == 0:
            kb.wait("tensor", [("gpsimd", a_ld[1]), ("vector", ln2_done)])
        for c in range(2):
            for t in range(4):
                te_xa = kb.op("tensor", lambda e, hc=hc, c=c, t=t: e.matmul(
                    pb[c * 4 + t][:, :], A_sb[c][:, hc * 128:(hc + 1) * 128],
                    LNyT[:, hc * T + t * 512: hc * T + (t + 1) * 512],
                    start=(hc == 0), stop=(hc == HC - 1)),
                    inc=(hc == HC - 1 and c == 1 and t == 3))
    a_xa = None
    for c in range(2):
        for t in range(4):
            a_xa = kb.op("scalar", lambda e, c=c, t=t: e.activation(
                xaT[c][:, t * 512:(t + 1) * 512], pb[c * 4 + t][:, :], AF.Copy,
                scale=wexp[:, c:c + 1]),
                waits=([("tensor", te_xa), ("vector", v_wexp)] if (c == 0 and t == 0)
                       else [("scalar", a_xa)]),
                inc=True)
    kb.free_banks(range(8), "scalar", a_xa)

    bf_ld = [kb.op("gpsimd", lambda e, c=c: e.dma_start(
        Bf_sb[c][:, :], d_Bft[c, :, :]), waits=[("vector", v_bc4b)], inc=True, dma=True)
        for c in range(2)]

    wldW = [None, None]
    ogrp = [None, None]
    ost_clk = None
    for oc in range(HC):
        sl = oc % 2
        qb = [4 * sl, 4 * sl + 1, 4 * sl + 2, 4 * sl + 3]
        w = [("tensor", ogrp[sl])] if oc >= 2 else [("tensor", te_xa)]
        wldW[sl] = kb.op("gpsimd", lambda e, oc=oc, sl=sl: e.dma_start(
            Wbuf[sl][:, :], d_Wt[oc, :, :]), waits=w, inc=True, dma=True)
        kb.wait("tensor", [("gpsimd", wldW[sl]), ("scalar", a_xa), ("gpsimd", bf_ld[1])])
        kb.use_banks("tensor", qb)
        g = None
        for hc in range(HC):
            for t in range(4):
                kb.op("tensor", lambda e, sl=sl, hc=hc, t=t: e.matmul(
                    pb[4 * sl + t][:, :], Wbuf[sl][:, hc * 128:(hc + 1) * 128],
                    LNyT[:, hc * T + t * 512: hc * T + (t + 1) * 512],
                    start=(hc == 0), stop=False))
        for c in range(2):
            for t in range(4):
                g = kb.op("tensor", lambda e, sl=sl, c=c, t=t, oc=oc: e.matmul(
                    pb[4 * sl + t][:, :], Bf_sb[c][:, oc * 128:(oc + 1) * 128],
                    xaT[c][:, t * 512:(t + 1) * 512],
                    start=False, stop=(c == 1)),
                    inc=(c == 1 and t == 3))
        ogrp[sl] = g
        fns = [
            lambda e, oc=oc: e.tensor_tensor(
                outst[:, :], LNyT[:, oc * T:(oc + 1) * T], ir2_bc[:, :], ALU.mult),
            lambda e: e.tensor_tensor(outst[:, :], outst[:, :], m2_bc[:, :], ALU.add),
        ]
        for t in range(4):
            fns.append(lambda e, sl=sl, t=t, oc=oc: e.scalar_tensor_tensor(
                outst[:, t * 512:(t + 1) * 512], pb[4 * sl + t][:, :],
                biasc_sb[:, oc:oc + 1], outst[:, t * 512:(t + 1) * 512],
                ALU.add, ALU.add))
        evw = [("tensor", g), ("vector", ln2_done)]
        if ost_clk:
            evw.append(("sync", ost_clk))
        evc = kb.chain("vector", fns, waits=evw)
        kb.free_banks(qb, "vector", evc)
        ost_clk = kb.op("sync", lambda e, oc=oc: e.dma_start(
            d_out[oc * 128:(oc + 1) * 128, :], outst[:, :]),
            waits=[("vector", evc)], inc=True, dma=True)

    kb.wait("sync", [("sync", ost_clk)])

    with nc.Block() as block:
        block.tensor(lambda e: [f(e) for f in kb.ops["tensor"]])
        block.vector(lambda e: [f(e) for f in kb.ops["vector"]])
        block.scalar(lambda e: [f(e) for f in kb.ops["scalar"]])
        block.gpsimd(lambda e: [f(e) for f in kb.ops["gpsimd"]])
        block.sync(lambda e: [f(e) for f in kb.ops["sync"]])
    es.close()
    return nc


def _prep_inputs(L0_attn_x, L0_attn_w_gate, L0_attn_w_up, L0_attn_w_down,
                 L0_ffn_W, L0_ffn_bias, L0_ffn_A, L0_ffn_Bf, L0_ffn_coeffs,
                 L0_ffn_scales):
    f = lambda a: np.asarray(a, dtype=np.float32)
    x2 = f(L0_attn_x).reshape(NCORES * T, H)
    wg = np.ascontiguousarray(
        f(L0_attn_w_gate).reshape(ICN, 128, HC, 128).transpose(0, 3, 2, 1)).reshape(ICN, 128, H)
    wu = np.ascontiguousarray(
        f(L0_attn_w_up).reshape(ICN, 128, HC, 128).transpose(0, 3, 2, 1)).reshape(ICN, 128, H)
    wd = np.ascontiguousarray(
        f(L0_attn_w_down).reshape(HC, 128, ICN, 128).transpose(0, 3, 2, 1)).reshape(HC, 128, I)
    Wt = np.ascontiguousarray(
        f(L0_ffn_W).reshape(HC, 128, HC, 128).transpose(0, 3, 2, 1)).reshape(HC, 128, H)
    At = np.ascontiguousarray(
        f(L0_ffn_A).reshape(256, H).reshape(2, 128, HC, 128).transpose(0, 3, 2, 1)).reshape(2, 128, H)
    Bf_flat = f(L0_ffn_Bf).transpose(1, 0, 2).reshape(H, 256)
    Bft = np.ascontiguousarray(
        Bf_flat.reshape(HC, 128, 2, 128).transpose(2, 3, 0, 1)).reshape(2, 128, H)
    biasc = np.ascontiguousarray(f(L0_ffn_bias).reshape(HC, 128).T)
    rep = np.zeros((16, 256), np.float32)
    for c in range(2):
        for p in range(128):
            rep[c * 8 + p // 16, c * 128 + p] = 1.0
    base = dict(
        wg=wg, wu=wu, wd=wd, Wt=Wt, At=At, Bft=Bft, biasc=biasc,
        coef=f(L0_ffn_coeffs).reshape(16, 1), scal=f(L0_ffn_scales).reshape(16, 1),
        onesc=np.ones((128, 1), np.float32),
        onescb=np.ones((128, 1), ml_dtypes.bfloat16),
        onesr=np.ones((1, 128), np.float32),
        rep=rep,
        ident=np.eye(128, dtype=np.float32),
    )
    maps = []
    for c in range(NCORES):
        m = dict(base)
        m["xT"] = np.ascontiguousarray(x2[c * T:(c + 1) * T].T)
        maps.append(m)
    return maps


def kernel(**inputs):
    if "nc" not in _CACHE:
        _CACHE["nc"] = build_program()
    nc = _CACHE["nc"]
    in_maps = _prep_inputs(**inputs)
    res = run_bass_kernel_spmd(nc, in_maps, core_ids=list(range(NCORES)))
    shards = [np.ascontiguousarray(r["outT"].T) for r in res.results]
    return np.concatenate(shards, axis=0).reshape(2, 8192, H)
